# version 04 (final)
# baseline (speedup 1.0000x reference)
import os
import numpy as np
import concourse.bass as bass
import concourse.mybir as mybir
from concourse.bass_utils import run_bass_kernel_spmd
from concourse.bass_core_rust import IndirectOffsetOnAxis
from contextlib import ExitStack

F32 = mybir.dt.float32
F32R = mybir.dt.float32r
BF16 = mybir.dt.bfloat16
U32 = mybir.dt.uint32
ALU = mybir.AluOpType
AF = mybir.ActivationFunctionType
AX = mybir.AxisListType

NCORES = 8
T = 8192
D = 2048
E = 4096
EC = E // NCORES
G = EC // 128
NST = 16
TC = T // NCORES
CCAP = 1536
NSL = CCAP // 128
NB = CCAP // 512
EPS = 1e-5
f1 = np.float32
RG = [list(range(NCORES))]


class Ct:
    def __init__(self, sem):
        self.sem = sem
        self.v = 0

    def inc(self, inst, n=1):
        inst.then_inc(self.sem, n)
        self.v += n
        return self.v

    def dma(self, inst):
        return self.inc(inst, 16)

    def wait(self, eng, v=None):
        eng.wait_ge(self.sem, self.v if v is None else v)


class QCt:

    def __init__(self, sems):
        self.sems = sems
        self.vals = [0] * len(sems)
        self.cur = 0
        self.dirty = False
        self.memo = {}

    def dma(self, inst):
        inst.then_inc(self.sems[self.cur], 16)
        self.vals[self.cur] += 16
        self.dirty = True

    def _advance(self):
        if self.dirty:
            self.cur = (self.cur + 1) % len(self.sems)
            self.dirty = False

    def token(self):
        tok = list(self.vals)
        self._advance()
        return tok

    def wait_token(self, eng, tok):
        memo = self.memo.setdefault(id(eng), [0] * len(self.sems))
        for i, (sem, v) in enumerate(zip(self.sems, tok)):
            if v > memo[i]:
                eng.wait_ge(sem, v)
                memo[i] = v

    def wait(self, eng):
        tok = list(self.vals)
        self._advance()
        self.wait_token(eng, tok)


def build_program():
    nc = bass.Bass("TRN2", target_bir_lowering=False, debug=False,
                   detect_race_conditions=False)

    resid = nc.dram_tensor("resid", [T, D], F32, kind="ExternalInput").ap()
    resid_sl = nc.dram_tensor("resid_sl", [TC, D], F32, kind="ExternalInput").ap()
    w_in_t = nc.dram_tensor("w_in_t", [D, EC], F32, kind="ExternalInput").ap()
    w_skip_t = nc.dram_tensor("w_skip_t", [D, EC], F32, kind="ExternalInput").ap()
    w_out_t = nc.dram_tensor("w_out_t", [EC, D], F32, kind="ExternalInput").ap()
    w_d1_t = nc.dram_tensor("w_d1_t", [EC, 128], F32, kind="ExternalInput").ap()
    w_d2_t = nc.dram_tensor("w_d2_t", [128, EC], F32, kind="ExternalInput").ap()
    w_bc_t = nc.dram_tensor("w_bc_t", [EC, 128], F32, kind="ExternalInput").ap()
    norm_w = nc.dram_tensor("norm_w", [1, D], F32, kind="ExternalInput").ap()
    conv_w = nc.dram_tensor("conv_w", [EC, 4], F32, kind="ExternalInput").ap()
    conv_b = nc.dram_tensor("conv_b", [EC, 1], F32, kind="ExternalInput").ap()
    b_d2 = nc.dram_tensor("b_d2", [EC, 1], F32, kind="ExternalInput").ap()
    a_log = nc.dram_tensor("a_log", [EC, NST], F32, kind="ExternalInput").ap()
    w_dvec = nc.dram_tensor("w_dvec", [EC, 1], F32, kind="ExternalInput").ap()
    router_w = nc.dram_tensor("router_w", [D, 128], F32, kind="ExternalInput").ap()
    gate_w = nc.dram_tensor("gate_w", [D, E], F32, kind="ExternalInput").ap()
    up_w = nc.dram_tensor("up_w", [D, E], F32, kind="ExternalInput").ap()
    down_w = nc.dram_tensor("down_w", [E, D], F32, kind="ExternalInput").ap()
    tril = nc.dram_tensor("tril", [128, 128], F32, kind="ExternalInput").ap()
    ident_h = nc.dram_tensor("ident_h", [128, 128], F32, kind="ExternalInput").ap()
    myexp = nc.dram_tensor("myexp", [1, 1], F32, kind="ExternalInput").ap()

    rows_out = nc.dram_tensor("rows_out", [D, CCAP], F32, kind="ExternalOutput").ap()
    ids_out = nc.dram_tensor("ids_out", [128, NSL], U32, kind="ExternalOutput").ap()

    xw_hi_sp = nc.dram_tensor("xw_hi_sp", [D, T], F32).ap()
    xw_lo_sp = nc.dram_tensor("xw_lo_sp", [D, T], F32).ap()
    xconv_sp = nc.dram_tensor("xconv_sp", [EC, T], F32).ap()
    skip_sp = nc.dram_tensor("skip_sp", [EC, T], F32).ap()
    ar_in = nc.dram_tensor("ar_in", [160, T], F32).ap()
    ar_out = nc.dram_tensor("ar_out", [160, T], F32).ap()
    rs_in = nc.dram_tensor("rs_in", [NCORES * D, TC], F32).ap()
    rs_out = nc.dram_tensor("rs_out", [D, TC], F32).ap()
    agA_in = nc.dram_tensor("agA_in", [1, TC], F32).ap()
    agA_out = nc.dram_tensor("agA_out", [NCORES, TC], F32).ap()
    agP_in = nc.dram_tensor("agP_in", [1, TC], F32).ap()
    agP_out = nc.dram_tensor("agP_out", [NCORES, TC], F32).ap()
    agx_in = nc.dram_tensor("agx_in", [TC, D], F32).ap()
    agx_out = nc.dram_tensor("agx_out", [T, D], F32).ap()
    idl = nc.dram_tensor("idl", [CCAP + 129, 1], U32).ap()
    probl = nc.dram_tensor("probl", [1, CCAP], F32).ap()
    x1f_sp = nc.dram_tensor("x1f_sp", [D, CCAP], F32).ap()
    ydbg_sp = nc.dram_tensor("ydbg_sp", [EC, T], F32).ap()
    hdbg_sp = nc.dram_tensor("hdbg_sp", [E, 512], F32).ap()
    ddbg_sp = nc.dram_tensor("ddbg_sp", [EC, T], F32).ap()
    gwhi = nc.dram_tensor("gwhi", [32 * 128 * 16, 128], BF16).ap()
    gwlo = nc.dram_tensor("gwlo", [32 * 128 * 16, 128], BF16).ap()
    uwhi = nc.dram_tensor("uwhi", [32 * 128 * 16, 128], BF16).ap()
    uwlo = nc.dram_tensor("uwlo", [32 * 128 * 16, 128], BF16).ap()
    dwhi = nc.dram_tensor("dwhi", [16 * 128 * 32, 128], BF16).ap()
    dwlo = nc.dram_tensor("dwlo", [16 * 128 * 32, 128], BF16).ap()

    es = ExitStack()
    with es:
        sbt = lambda name, shape, dt=F32: es.enter_context(nc.sbuf_tensor(name, shape, dt))
        ident = sbt("ident", [128, 128])
        cw = sbt("cw", [128, G * 4])
        cb = sbt("cb", [128, G])
        bd2 = sbt("bd2", [128, G])
        acol = sbt("acol", [128, G * NST])
        wdc = sbt("wdc", [128, G])
        nwv = sbt("nwv", [128, 16])
        wd2h = sbt("wd2h", [128, EC], F32R)
        wd2l = sbt("wd2l", [128, EC], F32R)

        pb = [es.enter_context(nc.psum_tensor(f"pb{i}", [128, 512], F32)) for i in range(8)]
        pba = [p.ap() for p in pb]

        sQ = QCt([es.enter_context(nc.semaphore(f"sQ{i}")) for i in range(6)])
        sG = QCt([es.enter_context(nc.semaphore(f"sG{i}")) for i in range(6)])
        sV = Ct(es.enter_context(nc.semaphore("sV")))
        sT = Ct(es.enter_context(nc.semaphore("sT")))
        sS = Ct(es.enter_context(nc.semaphore("sS")))
        sC = Ct(es.enter_context(nc.semaphore("sC")))
        sF = Ct(es.enter_context(nc.semaphore("sF")))

        sy, sc, ve, te, gp = nc.sync, nc.scalar, nc.vector, nc.tensor, nc.gpsimd
        ALL_ENG = (sy, sc, ve, te, gp)
        ALL_CT = (sV, sT, sS, sF)

        def sfence():
            sc.wait_ge(sS.sem, sS.v)

        def vf(inst):
            sV.inc(inst)
            ve.wait_ge(sV.sem, sV.v)

        def fence_all(engines=ALL_ENG):
            qq = sQ.token()
            gg = sG.token()
            for eng in engines:
                sQ.wait_token(eng, qq)
                sG.wait_token(eng, gg)
                for ct in ALL_CT:
                    if ct.v:
                        eng.wait_ge(ct.sem, ct.v)

        sQ.dma(sy.dma_start(out=ident[:, :], in_=ident_h[:, :]))
        for g in range(G):
            sQ.dma(sy.dma_start(out=cw[:, 4 * g:4 * g + 4], in_=conv_w[g * 128:(g + 1) * 128, :]))
            sQ.dma(sy.dma_start(out=cb[:, g:g + 1], in_=conv_b[g * 128:(g + 1) * 128, 0:1]))
            sQ.dma(sy.dma_start(out=bd2[:, g:g + 1], in_=b_d2[g * 128:(g + 1) * 128, 0:1]))
            sQ.dma(sy.dma_start(out=wdc[:, g:g + 1], in_=w_dvec[g * 128:(g + 1) * 128, 0:1]))
            sQ.dma(sy.dma_start(out=acol[:, g * NST:(g + 1) * NST],
                                in_=a_log[g * 128:(g + 1) * 128, :]))
        for k in range(16):
            sQ.dma(sy.dma_start(out=nwv[:, k:k + 1],
                                in_=norm_w[0:1, k * 128:(k + 1) * 128].transpose([1, 0])))
        sQ.wait(sc)
        sS.inc(sc.activation(acol[:, :], acol[:, :], AF.Exp))
        sfence()
        sS.inc(sc.mul(acol[:, :], acol[:, :], -1.0))

        with nc.sbuf_tensor("tset", [128, EC], F32) as tset:
            def split_small(dst_h, dst_l, src_ap, cols):
                sQ.dma(sy.dma_start(out=tset[:, 0:cols], in_=src_ap))
                sQ.wait(ve)
                ve.tensor_copy(dst_h[:, 0:cols], tset[:, 0:cols])
                sV.inc(ve.tensor_tensor(dst_l[:, 0:cols], tset[:, 0:cols], dst_h[:, 0:cols],
                                        ALU.subtract))
                sV.wait(sy)

            split_small(wd2h, wd2l, w_d2_t[:, :], EC)

        trp_tok = {0: 0, 1: 0}

        def tr128(src_slice, bank, consume):
            if trp_tok[bank]:
                te.wait_ge(sV.sem, trp_tok[bank])
            sT.inc(te.transpose(pba[bank][:, 0:128], src_slice, ident[:, :]))
            sT.wait(sc)
            sT.wait(ve)
            trp_tok[bank] = consume(pba[bank][:, 0:128])

        def consume_split(dsth, dstl):
            def fn(ps):
                sS.inc(sc.copy(dsth, ps))
                sS.wait(ve)
                sV.inc(ve.tensor_tensor(dstl, ps, dsth, ALU.subtract))
                return sV.v
            return fn

        def consume_copy(dst):
            def fn(ps):
                sV.inc(ve.tensor_copy(dst, ps))
                return sV.v
            return fn

        def norm_tile(src_tile, dst_tile, colsx, both_ln_rms):
            vf(ve.reduce_sum(colsx[:, 0:1], src_tile, axis=AX.X))
            sS.inc(sc.activation(dst_tile, src_tile, AF.Square))
            sS.wait(ve)
            vf(ve.reduce_sum(colsx[:, 1:2], dst_tile, axis=AX.X))
            vf(ve.tensor_scalar(colsx[:, 2:3], colsx[:, 0:1], 1.0 / D, None, ALU.mult))
            vf(ve.tensor_scalar(colsx[:, 3:4], colsx[:, 1:2], 1.0 / D, None, ALU.mult))
            vf(ve.tensor_tensor(colsx[:, 4:5], colsx[:, 2:3], colsx[:, 2:3], ALU.mult))
            vf(ve.tensor_tensor(colsx[:, 3:4], colsx[:, 3:4], colsx[:, 4:5], ALU.subtract))
            sV.wait(sc)
            sS.inc(sc.activation(colsx[:, 5:6], colsx[:, 3:4], AF.Ln, bias=f1(EPS)))
            sfence()
            sS.inc(sc.activation(colsx[:, 5:6], colsx[:, 5:6], AF.Exp, scale=f1(-0.5)))
            sS.wait(ve)
            sV.inc(ve.tensor_scalar(dst_tile, src_tile, colsx[:, 2:3], colsx[:, 5:6],
                                    ALU.subtract, ALU.mult))
            if not both_ln_rms:
                return
            sV.wait(sc)
            sS.inc(sc.activation(src_tile, dst_tile, AF.Square))
            sS.wait(ve)
            vf(ve.reduce_sum(colsx[:, 6:7], src_tile, axis=AX.X))
            sV.wait(sc)
            sS.inc(sc.activation(colsx[:, 7:8], colsx[:, 6:7], AF.Ln, bias=f1(EPS), scale=f1(1.0 / D)))
            sfence()
            sS.inc(sc.activation(colsx[:, 7:8], colsx[:, 7:8], AF.Exp, scale=f1(-0.5)))
            sS.wait(ve)
            sV.inc(ve.tensor_scalar(dst_tile, dst_tile, colsx[:, 7:8], None, ALU.mult))

        NT1, TT = 16, 512
        pes = ExitStack()
        with pes:
            winh = pes.enter_context(nc.sbuf_tensor("winh", [128, 16 * EC], F32R))
            winl = pes.enter_context(nc.sbuf_tensor("winl", [128, 16 * EC], F32R))
            with nc.sbuf_tensor("wtmp", [128, 16 * EC], F32) as wtmp:
                for k in range(16):
                    sQ.dma(sy.dma_start(out=wtmp[:, k * EC:(k + 1) * EC],
                                        in_=w_in_t[k * 128:(k + 1) * 128, :]))
                sQ.wait(ve)
                for k in range(16):
                    w = wtmp[:, k * EC:(k + 1) * EC]
                    ve.tensor_scalar(w, w, nwv[:, k:k + 1], None, ALU.mult)
                    ve.tensor_copy(winh[:, k * EC:(k + 1) * EC], w)
                    ve.tensor_tensor(winl[:, k * EC:(k + 1) * EC], w,
                                     winh[:, k * EC:(k + 1) * EC], ALU.subtract)
                sV.inc(ve.engine_nop())
                sV.wait(sy)
            wd1h = pes.enter_context(nc.sbuf_tensor("wd1h", [128, G * 128], F32R))
            wd1l = pes.enter_context(nc.sbuf_tensor("wd1l", [128, G * 128], F32R))
            wbch = pes.enter_context(nc.sbuf_tensor("wbch", [128, G * 128], F32R))
            wbcl = pes.enter_context(nc.sbuf_tensor("wbcl", [128, G * 128], F32R))
            with nc.sbuf_tensor("tse2", [128, 128], F32) as tse2:
                def split_small2(dst_h, dst_l, src_ap, cols):
                    sQ.dma(sy.dma_start(out=tse2[:, 0:cols], in_=src_ap))
                    sQ.wait(ve)
                    ve.tensor_copy(dst_h[:, 0:cols], tse2[:, 0:cols])
                    sV.inc(ve.tensor_tensor(dst_l[:, 0:cols], tse2[:, 0:cols], dst_h[:, 0:cols],
                                            ALU.subtract))
                    sV.wait(sy)
                for g in range(G):
                    split_small2(wd1h[:, g * 128:(g + 1) * 128], wd1l[:, g * 128:(g + 1) * 128],
                                 w_d1_t[g * 128:(g + 1) * 128, :], 128)
                    split_small2(wbch[:, g * 128:(g + 1) * 128], wbcl[:, g * 128:(g + 1) * 128],
                                 w_bc_t[g * 128:(g + 1) * 128, :], 128)
            rtile = pes.enter_context(nc.sbuf_tensor("rtile", [128, D], F32))
            xlt = pes.enter_context(nc.sbuf_tensor("xlt", [128, D], F32))
            xwh = pes.enter_context(nc.sbuf_tensor("xwh", [128, 16 * TT], F32R))
            xwl = pes.enter_context(nc.sbuf_tensor("xwl", [128, 16 * TT], F32R))
            xinb = pes.enter_context(nc.sbuf_tensor("xinb", [128, G * (TT + 3)], F32))
            xcbuf = pes.enter_context(nc.sbuf_tensor("xcbuf", [128, G * TT], F32))
            xch = pes.enter_context(nc.sbuf_tensor("xch", [128, G * TT], F32R))
            xcl = pes.enter_context(nc.sbuf_tensor("xcl", [128, G * TT], F32R))
            zev = pes.enter_context(nc.sbuf_tensor("zev", [128, TT], F32))
            bcev32 = pes.enter_context(nc.sbuf_tensor("bcev32", [32, TT], F32))
            colsa = pes.enter_context(nc.sbuf_tensor("colsa", [128, 8], F32))

            ve.memset(xinb[:, :], 0.0)
            sV.inc(ve.engine_nop())
            fence_all()

            for tt in range(NT1):
                t0 = tt * TT
                for blk in range(4):
                    tb = t0 + blk * 128
                    fence_all((sy, sc, ve))
                    sQ.dma(sy.dma_start(out=rtile[:, :], in_=resid[tb:tb + 128, :]))
                    sQ.wait(ve)
                    sQ.wait(sc)
                    norm_tile(rtile[:, :], xlt[:, :], colsa, True)
                    sV.wait(te)
                    for dk in range(16):
                        o = dk * TT + blk * 128
                        tr128(xlt[:, dk * 128:(dk + 1) * 128], dk % 2,
                              consume_split(xwh[:, o:o + 128], xwl[:, o:o + 128]))
                fence_all((gp,))
                for dk in range(16):
                    sG.dma(gp.dma_start(out=xw_hi_sp[dk * 128:(dk + 1) * 128, t0:t0 + TT],
                                        in_=xwh[:, dk * TT:(dk + 1) * TT].bitcast(F32)))
                    sG.dma(gp.dma_start(out=xw_lo_sp[dk * 128:(dk + 1) * 128, t0:t0 + TT],
                                        in_=xwl[:, dk * TT:(dk + 1) * TT].bitcast(F32)))
                fence_all((te,))
                for g in range(G):
                    first = True
                    for k in range(16):
                        wh = winh[:, k * EC + g * 128: k * EC + (g + 1) * 128]
                        wl = winl[:, k * EC + g * 128: k * EC + (g + 1) * 128]
                        xh = xwh[:, k * TT:(k + 1) * TT]
                        xl = xwl[:, k * TT:(k + 1) * TT]
                        te.matmul(pba[4 + g][:, :], wh, xh, start=first, stop=False); first = False
                        te.matmul(pba[4 + g][:, :], wh, xl, start=False, stop=False)
                        last = te.matmul(pba[4 + g][:, :], wl, xh, start=False, stop=(k == 15))
                    sT.inc(last)
                sT.wait(ve)
                for g in range(G):
                    ve.tensor_copy(xinb[:, g * (TT + 3) + 3:(g + 1) * (TT + 3)], pba[4 + g][:, :])
                sV.inc(ve.engine_nop())
                for g in range(G):
                    base = g * (TT + 3)
                    acc = xcbuf[:, g * TT:(g + 1) * TT]
                    ve.tensor_scalar(acc, xinb[:, base:base + TT], cw[:, 4 * g:4 * g + 1],
                                     None, ALU.mult)
                    for j in range(1, 4):
                        ve.scalar_tensor_tensor(acc, xinb[:, base + j:base + j + TT],
                                                cw[:, 4 * g + j:4 * g + j + 1], acc, ALU.mult, ALU.add)
                    sV.inc(ve.engine_nop())
                    sV.wait(sc)
                    sS.inc(sc.activation(acc, acc, AF.Silu, bias=cb[:, g:g + 1]))
                    sS.wait(ve)
                    ve.tensor_copy(xch[:, g * TT:(g + 1) * TT], acc)
                    ve.tensor_tensor(xcl[:, g * TT:(g + 1) * TT], acc, xch[:, g * TT:(g + 1) * TT],
                                     ALU.subtract)
                    sV.inc(ve.tensor_copy(xinb[:, base:base + 3], xinb[:, base + TT:base + TT + 3]))
                fence_all((gp,))
                for g in range(G):
                    sG.dma(gp.dma_start(out=xconv_sp[g * 128:(g + 1) * 128, t0:t0 + TT],
                                        in_=xcbuf[:, g * TT:(g + 1) * TT]))
                fence_all((te,))
                first = True
                for g in range(G):
                    xh = xch[:, g * TT:(g + 1) * TT]
                    xl = xcl[:, g * TT:(g + 1) * TT]
                    for (bank, wh_t, wl_t) in ((2, wd1h, wd1l), (3, wbch, wbcl)):
                        wh = wh_t[:, g * 128:(g + 1) * 128]
                        wl = wl_t[:, g * 128:(g + 1) * 128]
                        te.matmul(pba[bank][:, :], wh, xh, start=first, stop=False)
                        te.matmul(pba[bank][:, :], wh, xl, start=False, stop=False)
                        last = te.matmul(pba[bank][:, :], wl, xh, start=False, stop=(g == G - 1))
                    first = False
                sT.inc(last)
                sT.wait(ve)
                if tt > 0:
                    sG.wait_token(ve, zev_tok)
                ve.tensor_copy(zev[:, :], pba[2][:, :])
                sV.inc(ve.tensor_copy(bcev32[:, :], pba[3][0:32, :]))
                trp_tok[0] = sV.v
                trp_tok[1] = sV.v
                sV.wait(gp)
                sG.dma(gp.dma_start(out=ar_in[0:128, t0:t0 + TT], in_=zev[:, :]))
                sG.dma(gp.dma_start(out=ar_in[128:160, t0:t0 + TT], in_=bcev32[:, :]))
                zev_tok = sG.token()
            a1_dma = sG.token()

        sG.wait_token(gp, a1_dma)
        sC.inc(gp.collective_compute("AllReduce", ALU.add, replica_groups=RG,
                                     ins=[ar_in.opt()], outs=[ar_out.opt()]))
        ar_done = sC.v

        pes = ExitStack()
        with pes:
            wskh = pes.enter_context(nc.sbuf_tensor("wskh", [128, 16 * EC], F32R))
            wskl = pes.enter_context(nc.sbuf_tensor("wskl", [128, 16 * EC], F32R))
            fence_all()
            with nc.sbuf_tensor("wtmp2", [128, 16 * EC], F32) as wtmp2:
                for k in range(16):
                    sQ.dma(sy.dma_start(out=wtmp2[:, k * EC:(k + 1) * EC],
                                        in_=w_skip_t[k * 128:(k + 1) * 128, :]))
                sQ.wait(ve)
                fence_all((ve,))
                for k in range(16):
                    w = wtmp2[:, k * EC:(k + 1) * EC]
                    ve.tensor_scalar(w, w, nwv[:, k:k + 1], None, ALU.mult)
                    ve.tensor_copy(wskh[:, k * EC:(k + 1) * EC], w)
                    ve.tensor_tensor(wskl[:, k * EC:(k + 1) * EC], w,
                                     wskh[:, k * EC:(k + 1) * EC], ALU.subtract)
                sV.inc(ve.engine_nop())
                sV.wait(sy)
            xwh2 = pes.enter_context(nc.sbuf_tensor("xwh2", [128, 16 * TT], F32R))
            xwl2 = pes.enter_context(nc.sbuf_tensor("xwl2", [128, 16 * TT], F32R))
            skev = pes.enter_context(nc.sbuf_tensor("skev", [128, G * TT], F32))

            sG.wait_token(sy, a1_dma)
            for tt in range(NT1):
                t0 = tt * TT
                fence_all((sy,))
                for dk in range(16):
                    sQ.dma(sy.dma_start(out=xwh2[:, dk * TT:(dk + 1) * TT].bitcast(F32),
                                        in_=xw_hi_sp[dk * 128:(dk + 1) * 128, t0:t0 + TT]))
                    sQ.dma(sy.dma_start(out=xwl2[:, dk * TT:(dk + 1) * TT].bitcast(F32),
                                        in_=xw_lo_sp[dk * 128:(dk + 1) * 128, t0:t0 + TT]))
                sQ.wait(te)
                fence_all((te,))
                for g in range(G):
                    first = True
                    for k in range(16):
                        wh = wskh[:, k * EC + g * 128: k * EC + (g + 1) * 128]
                        wl = wskl[:, k * EC + g * 128: k * EC + (g + 1) * 128]
                        xh = xwh2[:, k * TT:(k + 1) * TT]
                        xl = xwl2[:, k * TT:(k + 1) * TT]
                        te.matmul(pba[4 + g][:, :], wh, xh, start=first, stop=False); first = False
                        te.matmul(pba[4 + g][:, :], wh, xl, start=False, stop=False)
                        last = te.matmul(pba[4 + g][:, :], wl, xh, start=False, stop=(k == 15))
                    sT.inc(last)
                sT.wait(ve)
                if tt > 0:
                    sG.wait_token(ve, skev_tok)
                for g in range(G):
                    ve.tensor_copy(skev[:, g * TT:(g + 1) * TT], pba[4 + g][:, :])
                sV.inc(ve.engine_nop())
                sV.wait(gp)
                for g in range(G):
                    sG.dma(gp.dma_start(out=skip_sp[g * 128:(g + 1) * 128, t0:t0 + TT],
                                        in_=skev[:, g * TT:(g + 1) * TT]))
                skev_tok = sG.token()
            a2_dma = sG.token()

        Q, TQ = 8, 1024
        pes = ExitStack()
        with pes:
            ztile = pes.enter_context(nc.sbuf_tensor("ztile", [128, TQ], F32))
            zhb = pes.enter_context(nc.sbuf_tensor("zhb", [128, TQ], F32R))
            zlb = pes.enter_context(nc.sbuf_tensor("zlb", [128, TQ], F32R))
            delta = pes.enter_context(nc.sbuf_tensor("delta", [128, G * TQ], F32))
            uu = pes.enter_context(nc.sbuf_tensor("uu", [128, G * TQ], F32))
            bbc = pes.enter_context(nc.sbuf_tensor("bbc", [128, TQ], F32))
            cbc = pes.enter_context(nc.sbuf_tensor("cbc", [128, TQ], F32))
            ddt = pes.enter_context(nc.sbuf_tensor("ddt", [128, 2 * TQ], F32))
            wwt = pes.enter_context(nc.sbuf_tensor("wwt", [128, 2 * TQ], F32))
            hht = pes.enter_context(nc.sbuf_tensor("hht", [128, 2 * TQ], F32))
            state = pes.enter_context(nc.sbuf_tensor("state", [128, G * NST], F32))
            xcq = pes.enter_context(nc.sbuf_tensor("xcq", [128, TQ], F32))
            skq = pes.enter_context(nc.sbuf_tensor("skq", [128, TQ], F32))
            yq = pes.enter_context(nc.sbuf_tensor("yq", [128, TQ], F32))
            yhb = pes.enter_context(nc.sbuf_tensor("yhb", [128, G * TQ], F32R))
            ylb = pes.enter_context(nc.sbuf_tensor("ylb", [128, G * TQ], F32R))
            woh = pes.enter_context(nc.sbuf_tensor("woh", [128, G * D], F32R))
            wol = pes.enter_context(nc.sbuf_tensor("wol", [128, G * D], F32R))
            woev = pes.enter_context(nc.sbuf_tensor("woev", [128, TQ], F32))

            fence_all()
            with nc.sbuf_tensor("wotm", [128, D], F32) as wotm:
                for k in range(G):
                    sQ.dma(sy.dma_start(out=wotm[:, :], in_=w_out_t[k * 128:(k + 1) * 128, :]))
                    sQ.wait(ve)
                    ve.tensor_copy(woh[:, k * D:(k + 1) * D], wotm[:, :])
                    sV.inc(ve.tensor_tensor(wol[:, k * D:(k + 1) * D], wotm[:, :],
                                            woh[:, k * D:(k + 1) * D], ALU.subtract))
                    sV.wait(sy)
            ve.memset(state[:, :], 0.0)
            sV.inc(ve.engine_nop())
            sC.wait(sy, ar_done)
            sG.wait_token(gp, a2_dma)

            for q in range(Q):
                t0 = q * TQ
                fence_all((sy,))
                sQ.dma(sy.dma_start(out=ztile[:, :], in_=ar_out[0:128, t0:t0 + TQ]))
                sQ.wait(ve)
                fence_all((ve,))
                ve.tensor_copy(zhb[:, :], ztile[:, :])
                sV.inc(ve.tensor_tensor(zlb[:, :], ztile[:, :], zhb[:, :], ALU.subtract))
                sV.wait(te)
                fence_all((te,))
                for g in range(G):
                    for half in range(2):
                        bank = 2 * (g % 2) + half
                        if q + g > 0:
                            te.wait_ge(sS.sem, sS.v)
                        w2h = wd2h[:, g * 128:(g + 1) * 128]
                        w2l = wd2l[:, g * 128:(g + 1) * 128]
                        zh_s = zhb[:, half * 512:(half + 1) * 512]
                        zl_s = zlb[:, half * 512:(half + 1) * 512]
                        te.matmul(pba[bank][:, :], w2h, zh_s, start=True, stop=False)
                        te.matmul(pba[bank][:, :], w2h, zl_s, start=False, stop=False)
                        sT.inc(te.matmul(pba[bank][:, :], w2l, zh_s, start=False, stop=True))
                        sT.wait(sc)
                        dsl = delta[:, g * TQ + half * 512: g * TQ + (half + 1) * 512]
                        sS.inc(sc.activation(dsl, pba[bank][:, :], AF.Exp, bias=bd2[:, g:g + 1]))
                        sfence()
                        sS.inc(sc.activation(dsl, dsl, AF.Ln, bias=f1(1.0)))
                    sV.wait(gp)
                    sG.dma(gp.dma_start(out=xcq[:, :], in_=xconv_sp[g * 128:(g + 1) * 128, t0:t0 + TQ]))
                    sG.wait(ve)
                    sS.wait(ve)
                    sV.inc(ve.tensor_tensor(uu[:, g * TQ:(g + 1) * TQ], delta[:, g * TQ:(g + 1) * TQ],
                                            xcq[:, :], ALU.mult))
                    if os.environ.get("BASS_DEBUG_DUMP"):
                        sV.wait(gp)
                        sG.dma(gp.dma_start(out=ddbg_sp[g * 128:(g + 1) * 128, t0:t0 + TQ],
                                            in_=delta[:, g * TQ:(g + 1) * TQ]))
                n_ms = sV.v
                for n in range(NST):
                    sy.wait_ge(sV.sem, n_ms)
                    sQ.dma(sy.dma_start(out=bbc[:, :],
                                        in_=ar_out[128 + n:129 + n, t0:t0 + TQ].partition_broadcast(128)))
                    sQ.dma(sy.dma_start(out=cbc[:, :],
                                        in_=ar_out[144 + n:145 + n, t0:t0 + TQ].partition_broadcast(128)))
                    sQ.wait(ve)
                    sQ.wait(sc)
                    for gb in range(2):
                        ve.wait_ge(sT.sem, sT.v)
                        sc.wait_ge(sV.sem, n_ms)
                        for gg in range(2):
                            g = 2 * gb + gg
                            sl = slice(g * TQ, (g + 1) * TQ)
                            ss = slice(gg * TQ, (gg + 1) * TQ)
                            sS.inc(sc.activation(ddt[:, ss], delta[:, sl], AF.Exp,
                                                 scale=acol[:, g * NST + n:g * NST + n + 1]))
                            ve.tensor_tensor(wwt[:, ss], uu[:, sl], bbc[:, :], ALU.mult)
                            sS.wait(ve)
                            sF.inc(ve.tensor_tensor_scan(hht[:, ss], ddt[:, ss], wwt[:, ss],
                                                         state[:, g * NST + n:g * NST + n + 1],
                                                         ALU.mult, ALU.add))
                        sF.wait(ve)
                        for gg in range(2):
                            g = 2 * gb + gg
                            ss = slice(gg * TQ, (gg + 1) * TQ)
                            vf(ve.tensor_copy(state[:, g * NST + n:g * NST + n + 1],
                                              hht[:, gg * TQ + TQ - 1:gg * TQ + TQ]))
                            ve.tensor_tensor(wwt[:, ss], hht[:, ss], cbc[:, :], ALU.mult)
                        sV.inc(ve.engine_nop())
                        n_ms = sV.v
                        sV.wait(te)
                        for gg in range(2):
                            g = 2 * gb + gg
                            for half in range(2):
                                last = te.matmul(pba[2 * g + half][:, :], ident[:, :],
                                                 wwt[:, gg * TQ + half * 512: gg * TQ + (half + 1) * 512],
                                                 start=(n == 0), stop=(n == NST - 1))
                        sT.inc(last)
                sT.wait(ve)
                fence_all((gp,))
                for g in range(G):
                    sG.dma(gp.dma_start(out=xcq[:, :], in_=xconv_sp[g * 128:(g + 1) * 128, t0:t0 + TQ]))
                    sG.dma(gp.dma_start(out=skq[:, :], in_=skip_sp[g * 128:(g + 1) * 128, t0:t0 + TQ]))
                    ve.tensor_copy(yq[:, 0:512], pba[2 * g][:, :])
                    ve.tensor_copy(yq[:, 512:1024], pba[2 * g + 1][:, :])
                    sG.wait(ve)
                    sG.wait(sc)
                    sS.inc(sc.activation(skq[:, :], skq[:, :], AF.Silu))
                    ve.scalar_tensor_tensor(yq[:, :], xcq[:, :], wdc[:, g:g + 1], yq[:, :],
                                            ALU.mult, ALU.add)
                    sS.wait(ve)
                    ve.tensor_tensor(yq[:, :], yq[:, :], skq[:, :], ALU.mult)
                    ve.tensor_copy(yhb[:, g * TQ:(g + 1) * TQ], yq[:, :])
                    sV.inc(ve.tensor_tensor(ylb[:, g * TQ:(g + 1) * TQ], yq[:, :],
                                            yhb[:, g * TQ:(g + 1) * TQ], ALU.subtract))
                    if os.environ.get("BASS_DEBUG_DUMP"):
                        sV.wait(gp)
                        sG.dma(gp.dma_start(out=ydbg_sp[g * 128:(g + 1) * 128, t0:t0 + TQ],
                                            in_=yq[:, :]))
                        sG.wait(ve)
                    if g < G - 1:
                        fence_all((gp,))
                uu_tok = sV.v
                sV.wait(te)
                wo_tok = {}
                for dk in range(16):
                    if q + dk > 0:
                        sG.wait_token(ve, woev_tok)
                    for half in range(2):
                        bank = 4 + 2 * (dk % 2) + half
                        if (dk % 2, half) in wo_tok:
                            te.wait_ge(sV.sem, wo_tok[(dk % 2, half)])
                        first = True
                        for g in range(G):
                            wh = woh[:, g * D + dk * 128: g * D + (dk + 1) * 128]
                            wl = wol[:, g * D + dk * 128: g * D + (dk + 1) * 128]
                            xh = yhb[:, g * TQ + half * 512: g * TQ + (half + 1) * 512]
                            xl = ylb[:, g * TQ + half * 512: g * TQ + (half + 1) * 512]
                            te.matmul(pba[bank][:, :], wh, xh, start=first, stop=False); first = False
                            te.matmul(pba[bank][:, :], wh, xl, start=False, stop=False)
                            last = te.matmul(pba[bank][:, :], wl, xh, start=False, stop=(g == G - 1))
                        sT.inc(last)
                        sT.wait(ve)
                        sV.inc(ve.tensor_copy(woev[:, half * 512:(half + 1) * 512], pba[bank][:, :]))
                        wo_tok[(dk % 2, half)] = sV.v
                    sV.wait(gp)
                    sG.dma(gp.dma_start(out=rs_in[q * D + dk * 128: q * D + (dk + 1) * 128, :],
                                        in_=woev[:, :]))
                    woev_tok = sG.token()
            b_dma = sG.token()

        sG.wait_token(gp, b_dma)
        sC.inc(gp.collective_compute("ReduceScatter", ALU.add, replica_groups=RG,
                                     ins=[rs_in.opt()], outs=[rs_out.opt()]))
        rs_done = sC.v

        pes = ExitStack()
        with pes:
            wst = pes.enter_context(nc.sbuf_tensor("wst", [128, E], F32))
            wsh = pes.enter_context(nc.sbuf_tensor("wsh", [128, E], BF16))
            wsl = pes.enter_context(nc.sbuf_tensor("wsl", [128, E], BF16))

            gu_h = gwhi.rearrange("(i p k) c -> i k p c", i=32, p=128, k=16)
            gu_l = gwlo.rearrange("(i p k) c -> i k p c", i=32, p=128, k=16)
            up_h = uwhi.rearrange("(i p k) c -> i k p c", i=32, p=128, k=16)
            up_l = uwlo.rearrange("(i p k) c -> i k p c", i=32, p=128, k=16)
            dn_h = dwhi.rearrange("(d p i) c -> d i p c", d=16, p=128, i=32)
            dn_l = dwlo.rearrange("(d p i) c -> d i p c", d=16, p=128, i=32)
            fence_all()
            for (src, dh4, dl4, nk, ncol) in ((gate_w, gu_h, gu_l, 16, E),
                                              (up_w, up_h, up_l, 16, E),
                                              (down_w, dn_h, dn_l, 32, D)):
                for kb in range(nk):
                    fence_all((sy, ve))
                    sQ.dma(sy.dma_start(out=wst[:, 0:ncol], in_=src[kb * 128:(kb + 1) * 128, :]))
                    sQ.wait(ve)
                    ve.tensor_copy(wsh[:, 0:ncol], wst[:, 0:ncol])
                    sV.inc(ve.tensor_tensor(wsl[:, 0:ncol], wst[:, 0:ncol], wsh[:, 0:ncol],
                                            ALU.subtract))
                    sV.wait(gp)
                    for ic in range(ncol // 128):
                        sG.dma(gp.dma_start(out=dh4[ic, kb], in_=wsh[:, ic * 128:(ic + 1) * 128]))
                        sG.dma(gp.dma_start(out=dl4[ic, kb], in_=wsl[:, ic * 128:(ic + 1) * 128]))
            wsp_dma = sG.token()

        pes = ExitStack()
        with pes:
            stile = pes.enter_context(nc.sbuf_tensor("stile", [128, D], F32))
            rtile2 = pes.enter_context(nc.sbuf_tensor("rtile2", [128, D], F32))
            x1t = pes.enter_context(nc.sbuf_tensor("x1t", [128, D], F32))
            x2t = pes.enter_context(nc.sbuf_tensor("x2t", [128, D], F32))
            x2fh = pes.enter_context(nc.sbuf_tensor("x2fh", [128, 16 * 128], F32R))
            x2fl = pes.enter_context(nc.sbuf_tensor("x2fl", [128, 16 * 128], F32R))
            cols2 = pes.enter_context(nc.sbuf_tensor("cols2", [128, 8], F32))
            rt = pes.enter_context(nc.sbuf_tensor("rt", [128, 8], F32))
            rwh = pes.enter_context(nc.sbuf_tensor("rwh", [128, 16 * 128], F32R))
            rwl = pes.enter_context(nc.sbuf_tensor("rwl", [128, 16 * 128], F32R))
            rtf = pes.enter_context(nc.sbuf_tensor("rtf", [128, 128], F32))
            amaxt = pes.enter_context(nc.sbuf_tensor("amaxt", [128, 8], F32))
            probt = pes.enter_context(nc.sbuf_tensor("probt", [128, 8], F32))
            eqs = pes.enter_context(nc.sbuf_tensor("eqs", [128, 8], F32))
            takn = pes.enter_context(nc.sbuf_tensor("takn", [128, 1], F32))
            tmpc = pes.enter_context(nc.sbuf_tensor("tmpc", [128, 1], F32))

            fence_all()
            with nc.sbuf_tensor("tse3", [128, 128], F32) as tse3:
                for k in range(16):
                    sQ.dma(sy.dma_start(out=tse3[:, :], in_=router_w[k * 128:(k + 1) * 128, :]))
                    sQ.wait(ve)
                    ve.tensor_copy(rwh[:, k * 128:(k + 1) * 128], tse3[:, :])
                    sV.inc(ve.tensor_tensor(rwl[:, k * 128:(k + 1) * 128], tse3[:, :],
                                            rwh[:, k * 128:(k + 1) * 128], ALU.subtract))
                    sV.wait(sy)
            sC.wait(sy, rs_done)
            for blk in range(8):
                fence_all((sy,))
                for dk in range(16):
                    sQ.dma(sy.dma_start(out=stile[:, dk * 128:(dk + 1) * 128],
                                        in_=rs_out[dk * 128:(dk + 1) * 128,
                                                   blk * 128:(blk + 1) * 128]))
                sQ.dma(sy.dma_start(out=rtile2[:, :], in_=resid_sl[blk * 128:(blk + 1) * 128, :]))
                sQ.wait(te)
                sQ.wait(ve)
                sQ.wait(sc)
                fence_all((te, ve, sc))
                for dk in range(16):
                    tr128(stile[:, dk * 128:(dk + 1) * 128], dk % 2,
                          consume_copy(x1t[:, dk * 128:(dk + 1) * 128]))
                norm_tile(rtile2[:, :], x2t[:, :], cols2, False)
                ve.tensor_add(x1t[:, :], x1t[:, :], x2t[:, :])
                sV.inc(ve.tensor_add(x1t[:, :], x1t[:, :], rtile2[:, :]))
                sV.wait(gp)
                sG.dma(gp.dma_start(out=agx_in[blk * 128:(blk + 1) * 128, :], in_=x1t[:, :]))
                sV.wait(sc)
                norm_tile(x1t[:, :], x2t[:, :], cols2, False)
                sV.wait(te)
                for dk in range(16):
                    tr128(x2t[:, dk * 128:(dk + 1) * 128], dk % 2,
                          consume_split(x2fh[:, dk * 128:(dk + 1) * 128],
                                        x2fl[:, dk * 128:(dk + 1) * 128]))
                te.wait_ge(sS.sem, sS.v)
                te.wait_ge(sV.sem, sV.v)
                first = True
                for dk in range(16):
                    lh = x2fh[:, dk * 128:(dk + 1) * 128]
                    ll = x2fl[:, dk * 128:(dk + 1) * 128]
                    rh = rwh[:, dk * 128:(dk + 1) * 128]
                    rl = rwl[:, dk * 128:(dk + 1) * 128]
                    te.matmul(pba[2][:, 0:128], rh, lh, start=first, stop=False); first = False
                    te.matmul(pba[2][:, 0:128], rh, ll, start=False, stop=False)
                    last = te.matmul(pba[2][:, 0:128], rl, lh, start=False, stop=(dk == 15))
                sT.inc(last)
                sT.wait(ve)
                sV.inc(ve.tensor_copy(rtf[:, :], pba[2][:, 0:128]))
                sV.wait(te)
                sT.inc(te.transpose(pba[3][:, 0:128], rtf[:, :], ident[:, :]))
                sT.wait(sc)
                sS.inc(sc.activation(rt[:, :], pba[3][:, 0:8], AF.Sigmoid))
                sS.wait(ve)
                vf(ve.reduce_max(probt[:, blk:blk + 1], rt[:, :], axis=AX.X))
                vf(ve.tensor_scalar(eqs[:, :], rt[:, :], probt[:, blk:blk + 1], None, ALU.is_equal))
                ve.memset(takn[:, :], 0.0)
                vf(ve.memset(amaxt[:, blk:blk + 1], 0.0))
                for e in range(8):
                    vf(ve.tensor_tensor(tmpc[:, :], eqs[:, e:e + 1], takn[:, :], ALU.subtract))
                    vf(ve.tensor_scalar_max(tmpc[:, :], tmpc[:, :], 0.0))
                    vf(ve.tensor_max(takn[:, :], takn[:, :], eqs[:, e:e + 1]))
                    if e > 0:
                        vf(ve.scalar_tensor_tensor(amaxt[:, blk:blk + 1], tmpc[:, :], float(e),
                                                   amaxt[:, blk:blk + 1], ALU.mult, ALU.add))
                sV.inc(ve.engine_nop())
            sV.wait(gp)
            with nc.allow_non_contiguous_dma(reason="tiny route rows"):
                sG.dma(gp.dma_start(out=agA_in[0:1, :].rearrange("o (f p) -> (o p) f", p=128),
                                    in_=amaxt[:, :]))
                sG.dma(gp.dma_start(out=agP_in[0:1, :].rearrange("o (f p) -> (o p) f", p=128),
                                    in_=probt[:, :]))
            sG.wait(gp)
            sC.inc(gp.collective_compute("AllGather", ALU.bypass, replica_groups=RG,
                                         ins=[agA_in.opt()], outs=[agA_out.opt()]))
            sC.inc(gp.collective_compute("AllGather", ALU.bypass, replica_groups=RG,
                                         ins=[agP_in.opt()], outs=[agP_out.opt()]))
            agr_done = sC.v
            sC.inc(gp.collective_compute("AllGather", ALU.bypass, replica_groups=RG,
                                         ins=[agx_in.opt()], outs=[agx_out.opt()]))
            agx_done = sC.v

        NF = T // 128
        pes = ExitStack()
        with pes:
            amall = pes.enter_context(nc.sbuf_tensor("amall", [128, NF], F32))
            mke = pes.enter_context(nc.sbuf_tensor("mke", [128, NF], F32))
            ones64 = pes.enter_context(nc.sbuf_tensor("ones64", [128, NF], F32))
            pcs = pes.enter_context(nc.sbuf_tensor("pcs", [128, NF], F32))
            cnt = pes.enter_context(nc.sbuf_tensor("cnt", [128, 1], F32))
            bse = pes.enter_context(nc.sbuf_tensor("bse", [128, 1], F32))
            trl = pes.enter_context(nc.sbuf_tensor("trl", [128, 128], F32))
            pos = pes.enter_context(nc.sbuf_tensor("pos", [128, NF], F32))
            posu = pes.enter_context(nc.sbuf_tensor("posu", [128, NF], U32))
            tid = pes.enter_context(nc.sbuf_tensor("tid", [128, NF], U32))
            mexp = pes.enter_context(nc.sbuf_tensor("mexp", [128, 1], F32))
            fillr = pes.enter_context(nc.sbuf_tensor("fillr", [1, CCAP + 129], U32))

            fence_all()
            sQ.dma(sy.dma_start(out=trl[:, :], in_=tril[:, :]))
            sQ.dma(sy.dma_start(out=mexp[:, :], in_=myexp.partition_broadcast(128)))
            sC.wait(gp, agr_done)
            sG.dma(gp.dma_start(out=amall[:, :],
                                in_=agA_out.rearrange("r (pp f) -> (r pp) f", pp=16)))
            gp.iota(tid[:, :], pattern=[[1, NF]], channel_multiplier=NF)
            gp.memset(fillr[:, :], T)
            sG.dma(gp.dma_start(out=idl.transpose([1, 0]), in_=fillr[:, :]))
            sG.wait(ve)
            sQ.wait(ve)
            ve.memset(ones64[:, :], 1.0)
            ve.tensor_scalar(mke[:, :], amall[:, :], mexp[:, 0:1], None, ALU.is_equal)
            sF.inc(ve.tensor_tensor_scan(pcs[:, :], ones64[:, :], mke[:, :], 0.0, ALU.mult, ALU.add))
            sF.wait(ve)
            sV.inc(ve.tensor_copy(cnt[:, :], pcs[:, NF - 1:NF]))
            sQ.wait(te)
            sV.wait(te)
            fence_all((te,))
            sT.inc(te.matmul(pba[3][:, 0:1], trl[:, :], cnt[:, :], start=True, stop=True))
            sT.wait(ve)
            vf(ve.tensor_copy(bse[:, :], pba[3][:, 0:1]))
            vf(ve.tensor_tensor(pos[:, :], pcs[:, :], mke[:, :], ALU.subtract))
            vf(ve.tensor_scalar(pos[:, :], pos[:, :], bse[:, 0:1], None, ALU.add))
            vf(ve.tensor_tensor(pos[:, :], pos[:, :], mke[:, :], ALU.mult))
            vf(ve.tensor_tensor(pcs[:, :], ones64[:, :], mke[:, :], ALU.subtract))
            vf(ve.tensor_scalar(pcs[:, :], pcs[:, :], float(CCAP), None, ALU.mult))
            vf(ve.tensor_tensor(pos[:, :], pos[:, :], pcs[:, :], ALU.add))
            sV.inc(ve.tensor_copy(posu[:, :], pos[:, :]))
            sV.wait(gp)
            for f in range(NF):
                sG.dma(gp.indirect_dma_start(
                    out=idl[:, 0:1], out_offset=IndirectOffsetOnAxis(posu[:, f:f + 1], axis=0),
                    in_=tid[:, f:f + 1], in_offset=None))
            comp_dma = sG.token()

        pes = ExitStack()
        with pes:
            idx = pes.enter_context(nc.sbuf_tensor("idx", [128, NSL], U32))
            pcol = pes.enter_context(nc.sbuf_tensor("pcol", [128, NSL], F32))
            x1g = pes.enter_context(nc.sbuf_tensor("x1g", [128, D], F32))
            x2g = pes.enter_context(nc.sbuf_tensor("x2g", [128, D], F32))
            colsd = pes.enter_context(nc.sbuf_tensor("colsd", [128, 8], F32))
            x2bh = pes.enter_context(nc.sbuf_tensor("x2bh", [128, 16 * 512], BF16))
            x2bl = pes.enter_context(nc.sbuf_tensor("x2bl", [128, 16 * 512], BF16))
            x1stg = pes.enter_context(nc.sbuf_tensor("x1stg", [128, 2 * 128], F32))
            x1fq = pes.enter_context(nc.sbuf_tensor("x1fq", [128, 512], F32))
            wguh = pes.enter_context(nc.sbuf_tensor("wguh", [128, 2 * 16 * 128], BF16))
            wgul = pes.enter_context(nc.sbuf_tensor("wgul", [128, 2 * 16 * 128], BF16))
            wuph = pes.enter_context(nc.sbuf_tensor("wuph", [128, 2 * 16 * 128], BF16))
            wupl = pes.enter_context(nc.sbuf_tensor("wupl", [128, 2 * 16 * 128], BF16))
            wdnh = pes.enter_context(nc.sbuf_tensor("wdnh", [128, 2 * 32 * 128], BF16))
            wdnl = pes.enter_context(nc.sbuf_tensor("wdnl", [128, 2 * 32 * 128], BF16))
            hbuf = pes.enter_context(nc.sbuf_tensor("hbuf", [128, 512], F32))
            hsth = pes.enter_context(nc.sbuf_tensor("hsth", [128, 32 * 512], BF16))
            hstl = pes.enter_context(nc.sbuf_tensor("hstl", [128, 32 * 512], BF16))
            pbcb = pes.enter_context(nc.sbuf_tensor("pbcb", [128, 512], F32))
            oev = pes.enter_context(nc.sbuf_tensor("oev", [128, 2 * 512], F32))

            gu_src_h = gwhi.rearrange("(i p k) c -> i p (k c)", i=32, p=128, k=16)
            gu_src_l = gwlo.rearrange("(i p k) c -> i p (k c)", i=32, p=128, k=16)
            up_src_h = uwhi.rearrange("(i p k) c -> i p (k c)", i=32, p=128, k=16)
            up_src_l = uwlo.rearrange("(i p k) c -> i p (k c)", i=32, p=128, k=16)
            dn_src_h = dwhi.rearrange("(d p i) c -> d p (i c)", d=16, p=128, i=32)
            dn_src_l = dwlo.rearrange("(d p i) c -> d p (i c)", d=16, p=128, i=32)

            fence_all()
            sG.wait_token(gp, comp_dma)
            for c in range(NSL):
                sG.dma(gp.dma_start(out=idx[:, c:c + 1], in_=idl[c * 128:(c + 1) * 128, 0:1]))
            sG.wait(gp)
            sG.wait(ve)
            vf(ve.memset(pcol[:, :], 0.0))
            sV.inc(ve.engine_nop())
            sV.wait(gp)
            for c in range(NSL):
                sG.dma(gp.indirect_dma_start(
                    out=pcol[:, c:c + 1], out_offset=None,
                    in_=agP_out.rearrange("r (t q) -> (r t) q", q=1),
                    in_offset=IndirectOffsetOnAxis(idx[:, c:c + 1], axis=0),
                    bounds_check=T - 1, oob_is_err=False))
            sG.dma(gp.dma_start(out=ids_out[:, :], in_=idx[:, :]))
            sG.wait(gp)
            with nc.allow_non_contiguous_dma(reason="tiny prob columns"):
                for c in range(NSL):
                    sG.dma(gp.dma_start(out=probl[0:1, c * 128:(c + 1) * 128].transpose([1, 0]),
                                        in_=pcol[:, c:c + 1]))
            probl_tok = sG.token()
            sG.wait_token(gp, probl_tok)
            sG.wait_token(sy, probl_tok)
            sG.wait_token(sy, wsp_dma)
            sG.wait_token(gp, wsp_dma)
            sC.wait(gp, agx_done)
            sC.wait(sy, agx_done)

            gu_tok = {}
            dn_tok = {}
            wg_tok = {}
            wd_tok = {}
            oev_tok = {}
            x1s_tok = {}
            for nb in range(NB):
                fence_all((ve, sc, gp, sy))
                for cc in range(4):
                    c = nb * 4 + cc
                    ve.memset(x1g[:, :], 0.0)
                    sV.inc(ve.engine_nop())
                    sV.wait(gp)
                    sG.dma(gp.indirect_dma_start(
                        out=x1g[:, :], out_offset=None,
                        in_=agx_out[:, :], in_offset=IndirectOffsetOnAxis(idx[:, c:c + 1], axis=0),
                        bounds_check=T - 1, oob_is_err=False))
                    sG.wait(gp)
                    sG.wait(ve)
                    sG.wait(sc)
                    norm_tile(x1g[:, :], x2g[:, :], colsd, False)
                    sV.wait(te)
                    for dk in range(16):
                        o = dk * 512 + cc * 128
                        tr128(x2g[:, dk * 128:(dk + 1) * 128], dk % 2,
                              consume_split(x2bh[:, o:o + 128], x2bl[:, o:o + 128]))
                        sb_ = (dk % 2) * 128
                        if (dk % 2) in x1s_tok:
                            sG.wait_token(ve, x1s_tok[dk % 2])
                        tr128(x1g[:, dk * 128:(dk + 1) * 128], dk % 2,
                              consume_copy(x1stg[:, sb_:sb_ + 128]))
                        sV.wait(gp)
                        sG.dma(gp.dma_start(
                            out=x1f_sp[dk * 128:(dk + 1) * 128, c * 128:(c + 1) * 128],
                            in_=x1stg[:, sb_:sb_ + 128]))
                        x1s_tok[dk % 2] = sG.token()
                sQ.dma(sy.dma_start(out=pbcb[:, :],
                                    in_=probl[0:1, nb * 512:(nb + 1) * 512].partition_broadcast(128)))
                te.wait_ge(sV.sem, sV.v)
                te.wait_ge(sS.sem, sS.v)
                for ic in range(32):
                    buf = ic % 2
                    wo = buf * 16 * 128
                    if buf in wg_tok:
                        sy.wait_ge(sT.sem, wg_tok[buf])
                    sQ.dma(sy.dma_start(out=wguh[:, wo:wo + 16 * 128], in_=gu_src_h[ic]))
                    sQ.dma(sy.dma_start(out=wgul[:, wo:wo + 16 * 128], in_=gu_src_l[ic]))
                    sQ.dma(sy.dma_start(out=wuph[:, wo:wo + 16 * 128], in_=up_src_h[ic]))
                    sQ.dma(sy.dma_start(out=wupl[:, wo:wo + 16 * 128], in_=up_src_l[ic]))
                    sQ.wait(te)
                    if buf in gu_tok:
                        te.wait_ge(sV.sem, gu_tok[buf])
                    bg, bu = 2 + 2 * buf, 3 + 2 * buf
                    firstg = True
                    for k in range(16):
                        wh = wguh[:, wo + k * 128:wo + (k + 1) * 128]
                        wl = wgul[:, wo + k * 128:wo + (k + 1) * 128]
                        xh = x2bh[:, k * 512:(k + 1) * 512]
                        xl = x2bl[:, k * 512:(k + 1) * 512]
                        te.matmul(pba[bg][:, :], wh, xh, start=firstg, stop=False)
                        te.matmul(pba[bg][:, :], wh, xl, start=False, stop=False)
                        te.matmul(pba[bg][:, :], wl, xh, start=False, stop=(k == 15))
                        wh = wuph[:, wo + k * 128:wo + (k + 1) * 128]
                        wl = wupl[:, wo + k * 128:wo + (k + 1) * 128]
                        te.matmul(pba[bu][:, :], wh, xh, start=firstg, stop=False)
                        te.matmul(pba[bu][:, :], wh, xl, start=False, stop=False)
                        last = te.matmul(pba[bu][:, :], wl, xh, start=False, stop=(k == 15))
                        firstg = False
                    sT.inc(last)
                    wg_tok[buf] = sT.v
                    sT.wait(sc)
                    sS.inc(sc.activation(hbuf[:, :], pba[bg][:, :], AF.Silu))
                    sS.wait(ve)
                    ve.tensor_tensor(hbuf[:, :], hbuf[:, :], pba[bu][:, :], ALU.mult)
                    ve.tensor_copy(hsth[:, ic * 512:(ic + 1) * 512], hbuf[:, :])
                    sV.inc(ve.tensor_tensor(hstl[:, ic * 512:(ic + 1) * 512], hbuf[:, :],
                                            hsth[:, ic * 512:(ic + 1) * 512], ALU.subtract))
                    gu_tok[buf] = sV.v
                if os.environ.get("BASS_DEBUG_DUMP") and nb == 0:
                    fence_all((gp,))
                    for ic in range(32):
                        sG.dma(gp.dma_start(out=hdbg_sp[ic * 128:(ic + 1) * 128, :],
                                            in_=hsth[:, ic * 512:(ic + 1) * 512].bitcast(BF16)
                                            if False else hsth[:, ic * 512:(ic + 1) * 512]))
                    sG.wait(ve)
                te.wait_ge(sV.sem, sV.v)
                for dk in range(16):
                    buf = dk % 2
                    wo = buf * 32 * 128
                    if buf in wd_tok:
                        sy.wait_ge(sT.sem, wd_tok[buf])
                    sQ.dma(sy.dma_start(out=wdnh[:, wo:wo + 32 * 128], in_=dn_src_h[dk]))
                    sQ.dma(sy.dma_start(out=wdnl[:, wo:wo + 32 * 128], in_=dn_src_l[dk]))
                    sQ.wait(te)
                    if buf in dn_tok:
                        te.wait_ge(sV.sem, dn_tok[buf])
                    bank = 6 + buf
                    first = True
                    for it in range(32):
                        wh = wdnh[:, wo + it * 128:wo + (it + 1) * 128]
                        wl = wdnl[:, wo + it * 128:wo + (it + 1) * 128]
                        xh = hsth[:, it * 512:(it + 1) * 512]
                        xl = hstl[:, it * 512:(it + 1) * 512]
                        te.matmul(pba[bank][:, :], wh, xh, start=first, stop=False); first = False
                        te.matmul(pba[bank][:, :], wh, xl, start=False, stop=False)
                        last = te.matmul(pba[bank][:, :], wl, xh, start=False, stop=(it == 31))
                    sT.inc(last)
                    wd_tok[buf] = sT.v
                    fence_all((gp,))
                    sG.dma(gp.dma_start(out=x1fq[:, :],
                                        in_=x1f_sp[dk * 128:(dk + 1) * 128,
                                                   nb * 512:(nb + 1) * 512]))
                    sT.wait(ve)
                    sQ.wait(ve)
                    sG.wait(ve)
                    if buf in oev_tok:
                        sG.wait_token(ve, oev_tok[buf])
                    osl = oev[:, buf * 512:(buf + 1) * 512]
                    ve.tensor_tensor(osl, pba[bank][:, :], pbcb[:, :], ALU.mult)
                    sV.inc(ve.tensor_tensor(osl, osl, x1fq[:, :], ALU.add))
                    dn_tok[buf] = sV.v
                    sV.wait(gp)
                    sG.dma(gp.dma_start(out=rows_out[dk * 128:(dk + 1) * 128,
                                                     nb * 512:(nb + 1) * 512],
                                        in_=osl))
                    oev_tok[buf] = sG.token()
            sG.wait(gp)
            sQ.wait(gp)

        if os.environ.get("BASS_DEBUG_DUMP"):
            dbg = {
                "d_xconv": (xconv_sp, [EC, T]),
                "d_skip": (skip_sp, [EC, T]),
                "d_ar": (ar_out, [160, T]),
                "d_rs": (rs_out, [D, TC]),
                "d_agx": (agx_out, [T, D]),
                "d_agA": (agA_out, [NCORES, TC]),
                "d_agP": (agP_out, [NCORES, TC]),
                "d_idl": (idl, [CCAP + 129, 1]),
                "d_y": (ydbg_sp, [EC, T]),
                "d_delta": (ddbg_sp, [EC, T]),
                "d_rsin": (rs_in, [NCORES * D, TC]),
                "d_probl": (probl, [1, CCAP]),
                "d_hdbg": (hdbg_sp, [E, 512]),
                "d_x1f": (x1f_sp, [D, CCAP]),
                "d_xwh": (xw_hi_sp, [D, T]),
                "d_xwl": (xw_lo_sp, [D, T]),
            }
            fence_all()
            for name, (src, shape) in dbg.items():
                if src.dtype == U32:
                    dst = nc.dram_tensor(name, shape, U32, kind="ExternalOutput").ap()
                else:
                    dst = nc.dram_tensor(name, shape, F32, kind="ExternalOutput").ap()
                sG.dma(gp.dma_start(out=dst, in_=src))
            sG.wait(gp)
    return nc


_NC_CACHE = [None]


def kernel(L0_attn_resid, L0_attn_norm_w, L0_attn_W_skip, L0_attn_W_in, L0_attn_conv_w,
           L0_attn_conv_b, L0_attn_W_d1, L0_attn_W_d2, L0_attn_b_d2, L0_attn_W_Bm,
           L0_attn_W_Cm, L0_attn_A_log, L0_attn_W_D, L0_attn_W_out, L0_ffn_router_w,
           L0_ffn_gate_w, L0_ffn_up_w, L0_ffn_down_w):
    x = np.ascontiguousarray(np.asarray(L0_attn_resid, dtype=np.float32).reshape(T, D))
    cont = np.ascontiguousarray
    f4 = np.float32
    tril_m = (np.arange(128)[:, None] < np.arange(128)[None, :]).astype(f4)

    def _rwpad(rw):
        w = np.zeros((D, 128), dtype=f4)
        w[:, 0:8] = rw
        return w

    def _wbc(bm, cm):
        w = np.zeros((EC, 128), dtype=f4)
        w[:, 0:NST] = bm
        w[:, NST:2 * NST] = cm
        return w
    ident_m = np.eye(128, dtype=f4)
    in_maps = []
    for c in range(NCORES):
        sl = slice(c * EC, (c + 1) * EC)
        in_maps.append({
            "resid": x,
            "resid_sl": cont(x[c * TC:(c + 1) * TC]),
            "w_in_t": cont(np.asarray(L0_attn_W_in, f4)[sl, :].T),
            "w_skip_t": cont(np.asarray(L0_attn_W_skip, f4)[sl, :].T),
            "w_out_t": cont(np.asarray(L0_attn_W_out, f4)[:, sl].T),
            "w_d1_t": cont(np.asarray(L0_attn_W_d1, f4)[:, sl].T),
            "w_d2_t": cont(np.asarray(L0_attn_W_d2, f4)[sl, :].T),
            "w_bc_t": _wbc(np.asarray(L0_attn_W_Bm, f4)[:, sl].T,
                           np.asarray(L0_attn_W_Cm, f4)[:, sl].T),
            "norm_w": cont(np.asarray(L0_attn_norm_w, f4).reshape(1, D)),
            "conv_w": cont(np.asarray(L0_attn_conv_w, f4)[sl, 0, :]),
            "conv_b": cont(np.asarray(L0_attn_conv_b, f4)[sl].reshape(EC, 1)),
            "b_d2": cont(np.asarray(L0_attn_b_d2, f4)[sl].reshape(EC, 1)),
            "a_log": cont(np.asarray(L0_attn_A_log, f4)[sl, :]),
            "w_dvec": cont(np.asarray(L0_attn_W_D, f4)[sl].reshape(EC, 1)),
            "router_w": _rwpad(np.asarray(L0_ffn_router_w, f4)),
            "gate_w": cont(np.asarray(L0_ffn_gate_w, f4)[c]),
            "up_w": cont(np.asarray(L0_ffn_up_w, f4)[c]),
            "down_w": cont(np.asarray(L0_ffn_down_w, f4)[c]),
            "tril": tril_m,
            "ident_h": ident_m,
            "myexp": np.array([[float(c)]], dtype=f4),
        })
    if _NC_CACHE[0] is None:
        _NC_CACHE[0] = build_program()
    nc = _NC_CACHE[0]
    res = run_bass_kernel_spmd(nc, in_maps, core_ids=list(range(NCORES)))
    out = np.zeros((T, D), dtype=f4)
    for c in range(NCORES):
        rows = np.asarray(res.results[c]["rows_out"])
        ids = np.asarray(res.results[c]["ids_out"]).T.reshape(-1)
        valid = ids < T
        out[ids[valid].astype(np.int64)] = rows.T[valid]
    return out.reshape(1, T, D)
